# version 01
# baseline (speedup 1.0000x reference)
import numpy as np
import ml_dtypes
import concourse.bass as bass
import concourse.mybir as mybir
from concourse.bass_utils import run_bass_kernel_spmd

F32 = mybir.dt.float32
BF16 = mybir.dt.bfloat16
AF = mybir.ActivationFunctionType
ALU = mybir.AluOpType
AX = mybir.AxisListType
BF16NP = ml_dtypes.bfloat16

NC = 8
D = 2048
NBK = 16
T = 8192
OWN = T // NC
LB = 1024
TOT = LB + OWN
PW = 512
SW = 512
NH = 16
HD = 128
F = 8192
EPS_LN = 1e-5
EPS_RMS = 1e-6
RSQ = float(1.0 / np.sqrt(128.0))
MOD_SCALE = 0.1


class Sem:
    def __init__(self, nc, name):
        self.h = nc.ctx.enter_context(nc.semaphore(name))
        self.v = 0

    def inc(self, inst, n):
        inst.then_inc(self.h, n)
        self.v += n
        return self.v


def r3(ap2d, inner):
    return ap2d.rearrange("p (a t) -> p a t", t=inner)


def hb3(dram_slice, p=128):
    return dram_slice.rearrange("(a p) t -> p a t", p=p)


def build_l1():
    nc = bass.Bass("TRN2", target_bir_lowering=False)
    dram = lambda n, s, d, k="ExternalInput": nc.dram_tensor(n, s, d, kind=k)

    xT = dram("xT", [D, TOT], F32)
    xTm1 = dram("xTm1", [128, NBK], F32)
    WkeyT = dram("WkeyT", [D, D], F32)
    WvalT = dram("WvalT", [D, D], F32)
    WrecT = dram("WrecT", [D, D], F32)
    WqT = dram("WqT", [D, D], F32)
    WkT = dram("WkT", [D, D], F32)
    WvT = dram("WvT", [D, D], F32)
    VEC = dram("VEC", [128, NBK * 8], F32)
    lbm = dram("lbm", [128, 2], F32)

    kinT = dram("kinT", [D, TOT], BF16, "Internal")
    vinT = dram("vinT", [D, TOT], BF16, "Internal")
    rinT = dram("rinT", [D, TOT], BF16, "Internal")
    kT_h = dram("kT_h", [D, TOT], F32, "Internal")
    vT_h = dram("vT_h", [D, TOT], F32, "Internal")
    rT_h = dram("rT_h", [D, TOT], BF16, "Internal")
    u1T_h = dram("u1T_h", [D, OWN], BF16, "Internal")

    u2T_o = dram("u2T_o", [D, OWN], BF16, "ExternalOutput")
    xpxnT_o = dram("xpxnT_o", [D, OWN], F32, "ExternalOutput")
    KT_o = dram("KT_o", [D, OWN], BF16, "ExternalOutput")
    QT_o = dram("QT_o", [D, OWN], BF16, "ExternalOutput")
    V_o = dram("V_o", [OWN, D], BF16, "ExternalOutput")
    rwkvrT_o = dram("rwkvrT_o", [D, OWN], BF16, "ExternalOutput")
    dbgs_o = dram("dbgs_o", [128, 12 * SW], F32, "ExternalOutput")

    from contextlib import ExitStack
    with ExitStack() as ctx:
        sb = lambda n, s, d: ctx.enter_context(nc.sbuf_tensor(n, s, d))

        vecs = sb("vecs", [128, NBK * 8], F32)
        lbt = sb("lbt", [128, 2], F32)
        onescol = sb("onescol", [128, 1], F32)
        onesrow = sb("onesrow", [1, 128], F32)
        xncol = sb("xncol", [128, NBK], F32)
        sc1 = sb("sc1", [1, 8], F32)
        scb = sb("scb", [128, 2], F32)
        wstrip = sb("wstrip", [128, NBK * 128], BF16)
        evac = sb("evac", [128, 2048], F32)
        evacb = sb("evacb", [128, 2048], BF16)

        s_x = Sem(nc, "s_x")
        s_w = Sem(nc, "s_w")
        s_m = Sem(nc, "s_m")
        s_pe = Sem(nc, "s_pe")
        s_ve = Sem(nc, "s_ve")
        s_se = Sem(nc, "s_se")
        s_o = Sem(nc, "s_o")

        with nc.Block() as b:
            @b.sync
            def _(e):
                s_x.inc(e.dma_start(out=vecs[:], in_=VEC[:, :]), 16)
                s_x.inc(e.dma_start(out=lbt[:], in_=lbm[:, :]), 16)
                s_x.inc(e.dma_start(out=xncol[:], in_=xTm1[:, :]), 16)

            @b.vector
            def _(e):
                e.wait_ge(s_x.h, s_x.v)
                e.tensor_scalar(onescol[:], VECs_dummy_ap0 := vecs[:, 0:1], 0.0, 1.0,
                                ALU.mult, ALU.add)
                s_ve.inc(e.tensor_scalar(onesrow[:], vecs[0:1, 0:128], 0.0, 1.0,
                                         ALU.mult, ALU.add), 1)

            @b.scalar
            def _(e):
                e.wait_ge(s_x.h, s_x.v)
                e.activation(vecs[:, 4 * NBK:5 * NBK], vecs[:, 4 * NBK:5 * NBK], AF.Exp)
                s_se.inc(e.mul(vecs[:, 4 * NBK:5 * NBK], vecs[:, 4 * NBK:5 * NBK], -1.0), 1)

        with ExitStack() as c0:
            p_c = c0.enter_context(nc.psum_tensor("p_c", [1, 2 * NBK], F32))
            p_cb = c0.enter_context(nc.psum_tensor("p_cb", [128, 4], F32))
            with nc.Block() as b:
                @b.scalar
                def _(e):
                    e.wait_ge(s_x.h, s_x.v)
                    s_se.inc(e.activation(vecs[:, 7 * NBK:8 * NBK], xncol[:], AF.Square), 1)

                @b.tensor
                def _(e):
                    e.wait_ge(s_se.h, s_se.v)
                    e.wait_ge(s_ve.h, s_ve.v)
                    e.matmul(p_c[0:1, 0:NBK], onescol[:], xncol[:], start=True, stop=True)
                    e.matmul(p_c[0:1, NBK:2 * NBK], onescol[:],
                             vecs[:, 7 * NBK:8 * NBK], start=True, stop=True)
                    s_pe.inc(e.matmul(p_cb[:, 3:4], onesrow[:], lbt[0:1, 0:1],
                                      start=True, stop=True), 1)

                @b.vector
                def _(e):
                    e.wait_ge(s_pe.h, s_pe.v)
                    e.reduce_sum(sc1[0:1, 0:1], p_c[0:1, 0:NBK], axis=AX.X)
                    e.reduce_sum(sc1[0:1, 1:2], p_c[0:1, NBK:2 * NBK], axis=AX.X)
                    e.tensor_scalar_mul(sc1[0:1, 0:1], sc1[0:1, 0:1], 1.0 / D)
                    e.tensor_scalar_mul(sc1[0:1, 1:2], sc1[0:1, 1:2], 1.0 / D)
                    e.tensor_mul(sc1[0:1, 2:3], sc1[0:1, 0:1], sc1[0:1, 0:1])
                    e.tensor_sub(sc1[0:1, 1:2], sc1[0:1, 1:2], sc1[0:1, 2:3])
                    s_ve.inc(e.tensor_scalar_add(sc1[0:1, 1:2], sc1[0:1, 1:2], EPS_LN), 1)

                @b.scalar
                def _(e):
                    e.wait_ge(s_ve.h, s_ve.v)
                    e.activation(sc1[0:1, 1:2], sc1[0:1, 1:2], AF.Ln)
                    s_se.inc(e.activation(sc1[0:1, 1:2], sc1[0:1, 1:2], AF.Exp, scale=-0.5), 1)

                @b.tensor
                def _(e):
                    e.wait_ge(s_se.h, s_se.v)
                    e.matmul(p_cb[:, 1:2], onesrow[:], sc1[0:1, 0:1], start=True, stop=True)
                    e.matmul(p_cb[:, 2:3], onesrow[:], sc1[0:1, 1:2], start=True, stop=True)
                    s_pe.inc(e.matmul(p_cb[:, 3:4], onesrow[:], lbt[0:1, 0:1],
                                      start=True, stop=True), 1)

                @b.vector
                def _(e):
                    e.wait_ge(s_pe.h, s_pe.v)
                    e.tensor_copy(scb[:, 0:1], p_cb[:, 1:2])
                    e.tensor_copy(scb[:, 1:2], p_cb[:, 2:3])
                    e.tensor_sub(xncol[:], xncol[:], scb[:, 0:1].broadcast_to([128, NBK]))
                    e.tensor_mul(xncol[:], xncol[:], scb[:, 1:2].broadcast_to([128, NBK]))
                    s_ve.inc(e.tensor_scalar_mul(xncol[:], xncol[:], lbt[:, 0:1]), 1)

        NW = TOT // PW
        OWN_W0 = LB // PW
        with ExitStack() as c0:
            sb0 = lambda n, s, d: c0.enter_context(nc.sbuf_tensor(n, s, d))
            xw = sb0("xw", [128, NBK * PW], F32)
            xnw = sb0("xnw", [128, NBK * (PW + 1)], F32)
            xsq = sb0("xsq", [128, 2 * PW], F32)
            stat = sb0("stat", [1, 8 * PW], F32)
            bc = sb0("bc", [128, 3 * PW], F32)
            t1s = sb0("t1s", [128, 2 * PW], F32)
            u1s = sb0("u1s", [128, NBK * PW], BF16)
            u2s = sb0("u2s", [128, NBK * PW], BF16)
            mixs = sb0("mixs", [128, 3 * NBK * PW], BF16)
            p_st = c0.enter_context(nc.psum_tensor("p_st", [1, 2 * PW], F32))
            p_bc = c0.enter_context(nc.psum_tensor("p_bc", [128, 3 * PW], F32))

            for w in range(NW):
                own = w >= OWN_W0
                wo = w * PW
                oo = wo - LB

                with nc.Block() as b:
                    @b.sync
                    def _(e, wo=wo):
                        e.wait_ge(s_ve.h, s_ve.v)
                        e.wait_ge(s_o.h, s_o.v)
                        s_x.inc(e.dma_start(out=r3(xw.ap(), PW),
                                            in_=hb3(xT[:, wo:wo + PW])), 16)

                se_base = s_se.v
                pe_base = s_pe.v
                with nc.Block() as b:
                    @b.scalar
                    def _(e):
                        e.wait_ge(s_x.h, s_x.v)
                        for a in range(NBK):
                            if a >= 2:
                                e.wait_ge(s_pe.h, pe_base + a - 1)
                            e.activation(xsq[:, (a % 2) * PW:(a % 2 + 1) * PW],
                                         xw[:, a * PW:(a + 1) * PW],
                                         AF.Square).then_inc(s_se.h, 1)

                    @b.tensor
                    def _(e):
                        e.wait_ge(s_x.h, s_x.v)
                        for a in range(NBK):
                            e.matmul(p_st[0:1, 0:PW], onescol[:], xw[:, a * PW:(a + 1) * PW],
                                     start=(a == 0), stop=(a == NBK - 1))
                            e.wait_ge(s_se.h, se_base + a + 1)
                            e.matmul(p_st[0:1, PW:2 * PW], onescol[:],
                                     xsq[:, (a % 2) * PW:(a % 2 + 1) * PW],
                                     start=(a == 0), stop=(a == NBK - 1)).then_inc(s_pe.h, 1)
                s_se.v = se_base + NBK
                s_pe.v = pe_base + NBK

                with nc.Block() as b:
                    @b.vector
                    def _(e):
                        e.wait_ge(s_pe.h, s_pe.v)
                        e.tensor_scalar_mul(stat[0:1, 0:PW], p_st[0:1, 0:PW], 1.0 / D)
                        e.tensor_scalar_mul(stat[0:1, PW:2 * PW], p_st[0:1, PW:2 * PW], 1.0 / D)
                        e.tensor_mul(stat[0:1, 7 * PW:8 * PW], stat[0:1, 0:PW], stat[0:1, 0:PW])
                        e.tensor_sub(stat[0:1, PW:2 * PW], stat[0:1, PW:2 * PW],
                                     stat[0:1, 7 * PW:8 * PW])
                        s_ve.inc(e.tensor_scalar_add(stat[0:1, 4 * PW:5 * PW],
                                                     stat[0:1, PW:2 * PW], EPS_LN), 1)

                    @b.scalar
                    def _(e):
                        e.wait_ge(s_ve.h, s_ve.v)
                        e.activation(stat[0:1, 5 * PW:6 * PW], stat[0:1, 4 * PW:5 * PW], AF.Ln)
                        e.activation(stat[0:1, 2 * PW:3 * PW], stat[0:1, 5 * PW:6 * PW],
                                     AF.Exp, scale=-0.5)
                        s_se.inc(e.activation(stat[0:1, 7 * PW:8 * PW], stat[0:1, 5 * PW:6 * PW],
                                              AF.Exp, scale=-1.0), 1)

                    @b.vector
                    def _(e):
                        e.wait_ge(s_se.h, s_se.v)
                        e.tensor_mul(stat[0:1, 6 * PW:7 * PW], stat[0:1, PW:2 * PW],
                                     stat[0:1, 7 * PW:8 * PW])
                        s_ve.inc(e.tensor_scalar_add(stat[0:1, 6 * PW:7 * PW],
                                                     stat[0:1, 6 * PW:7 * PW], EPS_RMS), 1)

                    @b.scalar
                    def _(e):
                        e.wait_ge(s_ve.h, s_ve.v)
                        e.activation(stat[0:1, 6 * PW:7 * PW], stat[0:1, 6 * PW:7 * PW], AF.Ln)
                        s_se.inc(e.activation(stat[0:1, 3 * PW:4 * PW], stat[0:1, 6 * PW:7 * PW],
                                              AF.Exp, scale=-0.5), 1)

                    @b.tensor
                    def _(e):
                        e.wait_ge(s_se.h, s_se.v)
                        e.matmul(p_bc[:, 0:PW], onesrow[:], stat[0:1, 0:PW], start=True, stop=True)
                        e.matmul(p_bc[:, PW:2 * PW], onesrow[:], stat[0:1, 2 * PW:3 * PW],
                                 start=True, stop=True)
                        s_pe.inc(e.matmul(p_bc[:, 2 * PW:3 * PW], onesrow[:],
                                          stat[0:1, 3 * PW:4 * PW], start=True, stop=True), 1)

                    @b.vector
                    def _(e):
                        e.wait_ge(s_pe.h, s_pe.v)
                        e.tensor_copy(bc[:, 0:PW], p_bc[:, 0:PW])
                        e.tensor_copy(bc[:, PW:2 * PW], p_bc[:, PW:2 * PW])
                        s_ve.inc(e.tensor_copy(bc[:, 2 * PW:3 * PW], p_bc[:, 2 * PW:3 * PW]), 1)

                with nc.Block() as b:
                    @b.vector
                    def _(e, w=w, own=own):
                        xn3 = r3(xnw.ap(), PW + 1)
                        if w == 0:
                            e.tensor_copy(xn3[:, :, 0:1], xncol[:].unsqueeze(2))
                        else:
                            e.tensor_copy(xn3[:, :, 0:1], xn3[:, :, PW:PW + 1])
                        for a in range(NBK):
                            xna = xnw[:, a * (PW + 1):(a + 1) * (PW + 1)]
                            xa = xw[:, a * PW:(a + 1) * PW]
                            e.tensor_sub(xna[:, 1:PW + 1], xa, bc[:, 0:PW])
                            e.tensor_mul(xna[:, 1:PW + 1], xna[:, 1:PW + 1], bc[:, PW:2 * PW])
                            if own:
                                e.tensor_mul(t1s[:, 0:PW], xna[:, 1:PW + 1], bc[:, 2 * PW:3 * PW])
                                e.tensor_scalar_mul(u1s[:, a * PW:(a + 1) * PW], t1s[:, 0:PW],
                                                    vecs[:, 5 * NBK + a:5 * NBK + a + 1])
                                e.tensor_scalar_mul(u2s[:, a * PW:(a + 1) * PW], t1s[:, 0:PW],
                                                    vecs[:, 6 * NBK + a:6 * NBK + a + 1])
                                e.tensor_add(xa, xa, xna[:, 1:PW + 1])
                        for a in range(NBK):
                            xna = xnw[:, a * (PW + 1):(a + 1) * (PW + 1)]
                            e.tensor_sub(t1s[:, PW:2 * PW], xna[:, 1:PW + 1], xna[:, 0:PW])
                            for mi in range(3):
                                e.scalar_tensor_tensor(
                                    mixs[:, (mi * NBK + a) * PW:(mi * NBK + a + 1) * PW],
                                    t1s[:, PW:2 * PW], vecs[:, mi * NBK + a:mi * NBK + a + 1],
                                    xna[:, 0:PW], ALU.mult, ALU.add)
                        s_ve.inc(e.engine_nop(), 1)

                with nc.Block() as b:
                    @b.sync
                    def _(e, wo=wo, oo=oo, own=own):
                        e.wait_ge(s_ve.h, s_ve.v)
                        for mi, tgt in enumerate([kinT, vinT, rinT]):
                            s_o.inc(e.dma_start(
                                out=hb3(tgt[:, wo:wo + PW]),
                                in_=r3(mixs[:, mi * NBK * PW:(mi + 1) * NBK * PW], PW)), 16)
                        if own:
                            s_o.inc(e.dma_start(out=hb3(u1T_h[:, oo:oo + PW]),
                                                in_=r3(u1s.ap(), PW)), 16)
                            s_o.inc(e.dma_start(out=hb3(u2T_o[:, oo:oo + PW]),
                                                in_=r3(u2s.ap(), PW)), 16)
                            s_o.inc(e.dma_start(out=hb3(xpxnT_o[:, oo:oo + PW]),
                                                in_=r3(xw.ap(), PW)), 16)

        with ExitStack() as c1:
            mixfull = c1.enter_context(nc.sbuf_tensor("mixfull", [128, NBK * TOT], BF16))
            p_a = c1.enter_context(nc.psum_tensor("p1_a", [128, 2048], F32))
            p_b = c1.enter_context(nc.psum_tensor("p1_b", [128, 2048], F32))
            NT = TOT // 512

            for mi, (msrc, wsrc, dsth, is_r) in enumerate([
                    (kinT, WkeyT, kT_h, False), (vinT, WvalT, vT_h, False),
                    (rinT, WrecT, rT_h, True)]):
                with nc.Block() as b:
                    @b.scalar
                    def _(e, msrc=msrc, mi=mi):
                        if mi == 0:
                            e.wait_ge(s_o.h, s_o.v)
                        else:
                            e.wait_ge(s_pe.h, s_pe.v)
                        for q in range(4):
                            s_m.inc(e.dma_start(
                                out=r3(mixfull[:, q * 4 * TOT:(q + 1) * 4 * TOT], TOT),
                                in_=hb3(msrc[:, :])[:, q * 4:(q + 1) * 4, :]), 16)

                for ac in range(NBK):
                    pp = [p_a, p_b][ac % 2]
                    with nc.Block() as b:
                        @b.gpsimd
                        def _(e, wsrc=wsrc, ac=ac, mi=mi):
                            if mi + ac > 0:
                                e.wait_ge(s_pe.h, s_pe.v)
                            s_w.inc(e.dma_start(out=r3(wstrip.ap(), 128),
                                                in_=hb3(wsrc[:, ac * 128:(ac + 1) * 128])), 16)

                        @b.tensor
                        def _(e, pp=pp):
                            e.wait_ge(s_w.h, s_w.v)
                            e.wait_ge(s_m.h, s_m.v)
                            e.wait_ge(s_o.h, s_o.v)
                            ins = None
                            for dc in range(NBK):
                                lhs = wstrip[:, dc * 128:(dc + 1) * 128]
                                for tc in range(NT):
                                    ins = e.matmul(
                                        pp[:, tc * 512:(tc + 1) * 512], lhs,
                                        mixfull[:, dc * TOT + tc * 512:dc * TOT + (tc + 1) * 512],
                                        start=(dc == 0), stop=(dc == NBK - 1))
                            s_pe.inc(ins, 1)

                        if not is_r:
                            @b.vector
                            def _(e, pp=pp):
                                e.wait_ge(s_pe.h, s_pe.v)
                                e.wait_ge(s_o.h, s_o.v)
                                s_ve.inc(e.tensor_copy(evac[:], pp[:]), 1)

                            @b.sync
                            def _(e, ac=ac, dsth=dsth):
                                e.wait_ge(s_ve.h, s_ve.v)
                                s_o.inc(e.dma_start(out=dsth[ac * 128:(ac + 1) * 128, :],
                                                    in_=evac[:]), 16)
                        else:
                            @b.scalar
                            def _(e, pp=pp):
                                e.wait_ge(s_pe.h, s_pe.v)
                                e.wait_ge(s_o.h, s_o.v)
                                s_se.inc(e.activation(evacb[:], pp[:], AF.Sigmoid), 1)

                            @b.sync
                            def _(e, ac=ac, dsth=dsth):
                                e.wait_ge(s_se.h, s_se.v)
                                s_o.inc(e.dma_start(out=dsth[ac * 128:(ac + 1) * 128, :],
                                                    in_=evacb[:]), 16)

        with ExitStack() as c2:
            u1f = c2.enter_context(nc.sbuf_tensor("u1f", [128, NBK * OWN], BF16))
            wvst = c2.enter_context(nc.sbuf_tensor("wvst", [128, NBK * 2048], BF16))
            p_a = c2.enter_context(nc.psum_tensor("p2_a", [128, 2048], F32))
            p_b = c2.enter_context(nc.psum_tensor("p2_b", [128, 2048], F32))

            with nc.Block() as b:
                @b.scalar
                def _(e):
                    e.wait_ge(s_o.h, s_o.v)
                    for q in range(2):
                        s_m.inc(e.dma_start(
                            out=r3(u1f[:, q * 8 * OWN:(q + 1) * 8 * OWN], OWN),
                            in_=hb3(u1T_h[:, :])[:, q * 8:(q + 1) * 8, :]), 16)

            for mi, (wsrc, dsto) in enumerate([(WkT, KT_o), (WqT, QT_o)]):
                for oc in range(NBK):
                    pp = [p_a, p_b][oc % 2]
                    with nc.Block() as b:
                        @b.gpsimd
                        def _(e, wsrc=wsrc, oc=oc, mi=mi):
                            e.wait_ge(s_pe.h, s_pe.v)
                            s_w.inc(e.dma_start(out=r3(wstrip.ap(), 128),
                                                in_=hb3(wsrc[:, oc * 128:(oc + 1) * 128])), 16)

                        @b.tensor
                        def _(e, pp=pp):
                            e.wait_ge(s_w.h, s_w.v)
                            e.wait_ge(s_m.h, s_m.v)
                            e.wait_ge(s_o.h, s_o.v)
                            ins = None
                            for dc in range(NBK):
                                lhs = wstrip[:, dc * 128:(dc + 1) * 128]
                                for tc in range(OWN // 512):
                                    ins = e.matmul(
                                        pp[:, tc * 512:(tc + 1) * 512], lhs,
                                        u1f[:, dc * OWN + tc * 512:dc * OWN + (tc + 1) * 512],
                                        start=(dc == 0), stop=(dc == NBK - 1))
                            s_pe.inc(ins, 1)

                        @b.scalar
                        def _(e, pp=pp):
                            e.wait_ge(s_pe.h, s_pe.v)
                            e.wait_ge(s_o.h, s_o.v)
                            s_se.inc(e.activation(evacb[:, 0:OWN], pp[:, 0:OWN], AF.Identity), 1)

                        @b.sync
                        def _(e, oc=oc, dsto=dsto):
                            e.wait_ge(s_se.h, s_se.v)
                            s_o.inc(e.dma_start(out=dsto[oc * 128:(oc + 1) * 128, :],
                                                in_=evacb[:, 0:OWN]), 16)

            with nc.Block() as b:
                @b.gpsimd
                def _(e):
                    e.wait_ge(s_pe.h, s_pe.v)
                    for dc in range(NBK):
                        s_w.inc(e.dma_start(out=wvst[:, dc * 2048:(dc + 1) * 2048],
                                            in_=WvT[dc * 128:(dc + 1) * 128, :]), 16)

            for tch in range(OWN // 128):
                pp = [p_a, p_b][tch % 2]
                with nc.Block() as b:
                    @b.tensor
                    def _(e, tch=tch, pp=pp):
                        e.wait_ge(s_w.h, s_w.v)
                        e.wait_ge(s_o.h, s_o.v)
                        ins = None
                        for dc in range(NBK):
                            lhs = u1f[:, dc * OWN + tch * 128:dc * OWN + (tch + 1) * 128]
                            for jc in range(4):
                                ins = e.matmul(
                                    pp[:, jc * 512:(jc + 1) * 512], lhs,
                                    wvst[:, dc * 2048 + jc * 512:dc * 2048 + (jc + 1) * 512],
                                    start=(dc == 0), stop=(dc == NBK - 1))
                        s_pe.inc(ins, 1)

                    @b.scalar
                    def _(e, pp=pp):
                        e.wait_ge(s_pe.h, s_pe.v)
                        e.wait_ge(s_o.h, s_o.v)
                        s_se.inc(e.activation(evacb[:], pp[:], AF.Identity), 1)

                    @b.sync
                    def _(e, tch=tch):
                        e.wait_ge(s_se.h, s_se.v)
                        s_o.inc(e.dma_start(out=V_o[tch * 128:(tch + 1) * 128, :],
                                            in_=evacb[:]), 16)

        with ExitStack() as c3:
            sb3 = lambda n, s, d: c3.enter_context(nc.sbuf_tensor(n, s, d))
            KBUFs = [sb3(f"KB{a}", [128, SW], F32) for a in range(NBK)]
            WBUFs = [sb3(f"WB{a}", [128, SW], F32) for a in range(NBK)]
            ST = [[[sb3(f"S{p}_{k}_{a}", [128, 1], F32) for a in range(NBK)]
                   for k in range(3)] for p in range(2)]
            vwnd = sb3("vwnd", [128, NBK * SW], F32)
            rwnd = sb3("rwnd", [128, NBK * SW], BF16)
            E1S = sb3("E1S", [128, SW], F32)
            E2S = sb3("E2S", [128, SW], F32)
            D1B = sb3("D1B", [128, SW], F32)
            MXc = sb3("MXc", [128, SW], F32)
            NMc = sb3("NMc", [128, SW], F32)
            DNc = sb3("DNc", [128, SW], F32)
            sg = sb3("sg", [128, 6 * SW], F32)
            rwv = sb3("rwv", [128, NBK * SW], BF16)

            with nc.Block() as b:
                @b.vector
                def _(e):
                    e.wait_ge(s_se.h, s_se.v)
                    for a in range(NBK):
                        wv = vecs[:, 4 * NBK + a:4 * NBK + a + 1]
                        e.tensor_scalar(WBUFs[a][:], evac[:, 0:SW], 0.0, wv,
                                        ALU.mult, ALU.add)
                        e.tensor_scalar(ST[0][0][a][:], vecs[:, 0:1], 0.0, -1e38,
                                        ALU.mult, ALU.add)
                        e.tensor_scalar(ST[0][1][a][:], vecs[:, 0:1], 0.0, 0.0,
                                        ALU.mult, ALU.add)
                        e.tensor_scalar(ST[0][2][a][:], vecs[:, 0:1], 0.0, 0.0,
                                        ALU.mult, ALU.add)
                    s_ve.inc(e.engine_nop(), 1)

            NSW = TOT // SW
            OWN_S0 = LB // SW
            for wi in range(NSW):
                own = True
                emit_out = wi >= OWN_S0
                maskw = wi == OWN_S0 - 1
                par = wi % 2
                wo = wi * SW
                oo = wo - LB
                EIMX = ST[par][0]
                EINM = ST[par][1]
                EIDN = ST[par][2]
                NIMX = ST[1 - par][0]
                NINM = ST[1 - par][1]
                NIDN = ST[1 - par][2]
                with nc.Block() as b:
                    @b.scalar
                    def _(e, wo=wo):
                        e.wait_ge(s_ve.h, s_ve.v)
                        e.wait_ge(s_o.h, s_o.v)
                        for a in range(NBK):
                            s_m.inc(e.dma_start(out=KBUFs[a][:],
                                                in_=kT_h[a * 128:(a + 1) * 128, wo:wo + SW]), 16)
                        s_m.inc(e.dma_start(out=r3(vwnd.ap(), SW),
                                            in_=hb3(vT_h[:, wo:wo + SW])), 16)
                        s_m.inc(e.dma_start(out=r3(rwnd.ap(), SW),
                                            in_=hb3(rT_h[:, wo:wo + SW])), 16)

                for a in range(NBK):
                    wv = vecs[:, 4 * NBK + a:4 * NBK + a + 1]
                    tfv = vecs[:, 3 * NBK + a:3 * NBK + a + 1]
                    with nc.Block() as b:
                        @b.vector
                        def _(e, a=a, wv=wv):
                            e.wait_ge(s_m.h, s_m.v)
                            e.wait_ge(s_se.h, s_se.v)
                            e.tensor_tensor_scan(MXc[:], WBUFs[a][:], KBUFs[a][:],
                                                 EIMX[a][:, 0:1], ALU.add, ALU.max)
                            e.tensor_scalar_add(E1S[:, 0:1], EIMX[a][:], wv)
                            e.tensor_scalar_add(E1S[:, 1:SW], MXc[:, 0:SW - 1], wv)
                            e.tensor_sub(E1S[:], E1S[:], MXc[:])
                            s_ve.inc(e.tensor_sub(E2S[:], KBUFs[a][:], MXc[:]), 1)

                        @b.scalar
                        def _(e):
                            e.wait_ge(s_ve.h, s_ve.v)
                            e.activation(E1S[:], E1S[:], AF.Exp)
                            s_se.inc(e.activation(E2S[:], E2S[:], AF.Exp), 1)

                        @b.vector
                        def _(e, a=a, own=own, tfv=tfv):
                            e.wait_ge(s_se.h, s_se.v)
                            va = vwnd[:, a * SW:(a + 1) * SW]
                            e.tensor_mul(D1B[:], E2S[:], va)
                            if own:
                                e.tensor_scalar_add(sg[:, 0:SW], KBUFs[a][:], tfv)
                                e.tensor_max(sg[:, SW:SW + 1], EIMX[a][:], sg[:, 0:1])
                                e.tensor_max(sg[:, SW + 1:2 * SW], MXc[:, 0:SW - 1],
                                             sg[:, 1:SW])
                                e.tensor_sub(sg[:, 2 * SW:2 * SW + 1], EIMX[a][:],
                                             sg[:, SW:SW + 1])
                                e.tensor_sub(sg[:, 2 * SW + 1:3 * SW], MXc[:, 0:SW - 1],
                                             sg[:, SW + 1:2 * SW])
                                e.tensor_sub(sg[:, 3 * SW:4 * SW], sg[:, 0:SW],
                                             sg[:, SW:2 * SW])
                            s_ve.inc(e.engine_nop(), 1)

                        @b.scalar
                        def _(e, own=own):
                            e.wait_ge(s_ve.h, s_ve.v)
                            if own:
                                ins = e.activation(sg[:, 2 * SW:4 * SW],
                                                   sg[:, 2 * SW:4 * SW], AF.Exp)
                            else:
                                ins = e.nop()
                            s_se.inc(ins, 1)

                        @b.vector
                        def _(e, a=a, own=own, maskw=maskw):
                            e.wait_ge(s_se.h, s_se.v)
                            e.tensor_tensor_scan(NMc[:], E1S[:], D1B[:], EINM[a][:, 0:1],
                                                 ALU.mult, ALU.add)
                            e.tensor_tensor_scan(DNc[:], E1S[:], E2S[:], EIDN[a][:, 0:1],
                                                 ALU.mult, ALU.add)
                            if maskw:
                                e.tensor_scalar(NIMX[a][:], MXc[:, SW - 1:SW],
                                                lbt[:, 0:1], lbt[:, 1:2], ALU.mult, ALU.add)
                                e.tensor_scalar_mul(NINM[a][:], NMc[:, SW - 1:SW], lbt[:, 0:1])
                                e.tensor_scalar_mul(NIDN[a][:], DNc[:, SW - 1:SW], lbt[:, 0:1])
                            else:
                                e.tensor_copy(NIMX[a][:], MXc[:, SW - 1:SW])
                                e.tensor_copy(NINM[a][:], NMc[:, SW - 1:SW])
                                e.tensor_copy(NIDN[a][:], DNc[:, SW - 1:SW])
                            if own:
                                va = vwnd[:, a * SW:(a + 1) * SW]
                                e.tensor_mul(sg[:, 0:SW], sg[:, 3 * SW:4 * SW], va)
                                e.tensor_mul(sg[:, 4 * SW:4 * SW + 1],
                                             sg[:, 2 * SW:2 * SW + 1], EINM[a][:])
                                e.tensor_mul(sg[:, 4 * SW + 1:5 * SW],
                                             sg[:, 2 * SW + 1:3 * SW], NMc[:, 0:SW - 1])
                                e.tensor_add(sg[:, 4 * SW:5 * SW], sg[:, 4 * SW:5 * SW],
                                             sg[:, 0:SW])
                                e.tensor_mul(sg[:, 5 * SW:5 * SW + 1],
                                             sg[:, 2 * SW:2 * SW + 1], EIDN[a][:])
                                e.tensor_mul(sg[:, 5 * SW + 1:6 * SW],
                                             sg[:, 2 * SW + 1:3 * SW], DNc[:, 0:SW - 1])
                                e.tensor_add(sg[:, 5 * SW:6 * SW], sg[:, 5 * SW:6 * SW],
                                             sg[:, 3 * SW:4 * SW])
                            s_ve.inc(e.engine_nop(), 1)

                        if own:
                            @b.scalar
                            def _(e):
                                e.wait_ge(s_ve.h, s_ve.v)
                                e.activation(sg[:, 5 * SW:6 * SW], sg[:, 5 * SW:6 * SW], AF.Ln)
                                s_se.inc(e.activation(sg[:, 5 * SW:6 * SW],
                                                      sg[:, 5 * SW:6 * SW],
                                                      AF.Exp, scale=-1.0), 1)

                            @b.vector
                            def _(e, a=a):
                                e.wait_ge(s_se.h, s_se.v)
                                ra = rwnd[:, a * SW:(a + 1) * SW]
                                e.tensor_mul(sg[:, 4 * SW:5 * SW], sg[:, 4 * SW:5 * SW],
                                             sg[:, 5 * SW:6 * SW])
                                s_ve.inc(e.tensor_mul(rwv[:, a * SW:(a + 1) * SW],
                                                      sg[:, 4 * SW:5 * SW], ra), 1)

                if emit_out:
                    with nc.Block() as b:
                        @b.sync
                        def _(e, oo=oo):
                            e.wait_ge(s_ve.h, s_ve.v)
                            s_o.inc(e.dma_start(out=hb3(rwkvrT_o[:, oo:oo + SW]),
                                                in_=r3(rwv.ap(), SW)), 16)

        with nc.Block() as b:
            @b.sync
            def _(e):
                e.wait_ge(s_o.h, s_o.v)

    return nc


def build_l2():
    nc = bass.Bass("TRN2", target_bir_lowering=False)
    dram = lambda n, s, d, k="ExternalInput": nc.dram_tensor(n, s, d, kind=k)

    KTs = dram("KTs", [2 * HD, T], BF16)
    QTs = dram("QTs", [2 * HD, T], BF16)
    Vs = dram("Vs", [T, 2 * HD], BF16)
    cmask = dram("cmask", [128, 896], BF16)
    OT_o = dram("OT_o", [2 * HD, T], BF16, "ExternalOutput")

    NQ = T // 512

    from contextlib import ExitStack
    with ExitStack() as ctx:
        sb = lambda n, s, d: ctx.enter_context(nc.sbuf_tensor(n, s, d))
        KT = sb("KT", [128, 2 * T], BF16)
        QT = sb("QT", [128, 2 * T], BF16)
        VV = sb("VV", [128, 2 * T], BF16)
        msk = sb("msk", [128, 896], BF16)
        sq = sb("sq", [128, 512], BF16)
        onescol = sb("onescol", [128, 1], BF16)
        ones128 = sb("ones128", [128, 128], BF16)
        onesrow = sb("onesrow", [1, 128], F32)
        n2 = sb("n2", [1, 4 * NQ], F32)
        csc = sb("csc", [1, 8 * NQ], F32)
        cbias = sb("cbias", [128, 2], F32)
        PT = sb("PT", [128, 2 * 512], BF16)
        lst = sb("lst", [128, 512], F32)
        ot = sb("ot", [128, 512], BF16)

        p_s = ctx.enter_context(nc.psum_tensor("p_s", [128, 2 * 512], F32))
        p_l = ctx.enter_context(nc.psum_tensor("p_l", [128, 512], F32))
        p_o = ctx.enter_context(nc.psum_tensor("p_o", [128, 512], F32))
        p_n = ctx.enter_context(nc.psum_tensor("p_n", [1, 512], F32))
        p_cb = ctx.enter_context(nc.psum_tensor("p_cb", [128, 4], F32))

        s_x = Sem(nc, "s_x")
        s_pe = Sem(nc, "s_pe")
        s_ve = Sem(nc, "s_ve")
        s_se = Sem(nc, "s_se")
        s_o = Sem(nc, "s_o")
        s_ex = Sem(nc, "s_ex")
        s_pv = Sem(nc, "s_pv")

        with nc.Block() as b:
            @b.sync
            def _(e):
                s_x.inc(e.dma_start(out=KT[:, 0:T], in_=KTs[0:128, :]), 16)
                s_x.inc(e.dma_start(out=KT[:, T:2 * T], in_=KTs[128:256, :]), 16)
                s_x.inc(e.dma_start(out=QT[:, 0:T], in_=QTs[0:128, :]), 16)
                s_x.inc(e.dma_start(out=QT[:, T:2 * T], in_=QTs[128:256, :]), 16)
                s_x.inc(e.dma_start(out=msk[:], in_=cmask[:, :]), 16)
                for h in range(2):
                    s_x.inc(e.dma_start(
                        out=r3(VV[:, h * T:(h + 1) * T], 128),
                        in_=Vs[:, h * HD:(h + 1) * HD].rearrange("(b p) h -> p b h", p=128)), 16)

            @b.vector
            def _(e):
                e.wait_ge(s_x.h, s_x.v)
                e.tensor_scalar(onescol[:], msk[:, 0:1], 0.0, 1.0, ALU.mult, ALU.add)
                e.tensor_scalar(ones128[:], msk[:, 0:128], 0.0, 1.0, ALU.mult, ALU.add)
                s_ve.inc(e.tensor_scalar(onesrow[:], msk[0:1, 0:128], 0.0, 1.0,
                                         ALU.mult, ALU.add), 1)

        for h in range(2):
            for which, src_t in enumerate([QT, KT]):
                for qc in range(NQ):
                    with nc.Block() as b:
                        @b.scalar
                        def _(e, h=h, qc=qc, src_t=src_t):
                            e.wait_ge(s_x.h, s_x.v)
                            e.wait_ge(s_pe.h, s_pe.v)
                            s_se.inc(e.activation(
                                sq[:], src_t[:, h * T + qc * 512:h * T + (qc + 1) * 512],
                                AF.Square), 1)

                        @b.tensor
                        def _(e):
                            e.wait_ge(s_se.h, s_se.v)
                            e.wait_ge(s_ve.h, s_ve.v)
                            s_pe.inc(e.matmul(p_n[0:1, :], onescol[:], sq[:],
                                              start=True, stop=True), 1)

                        @b.vector
                        def _(e, h=h, which=which, qc=qc):
                            e.wait_ge(s_pe.h, s_pe.v)
                            idx = (h * 2 + which) * NQ + qc
                            s_ve.inc(e.reduce_max(n2[0:1, idx:idx + 1], p_n[0:1, :],
                                                  axis=AX.X), 1)

        with nc.Block() as b:
            @b.vector
            def _(e):
                e.wait_ge(s_ve.h, s_ve.v)
                for h in range(2):
                    kbase = (h * 2 + 1) * NQ
                    for qc in range(1, NQ):
                        e.tensor_max(n2[0:1, kbase + qc:kbase + qc + 1],
                                     n2[0:1, kbase + qc:kbase + qc + 1],
                                     n2[0:1, kbase + qc - 1:kbase + qc])
                    qbase = (h * 2) * NQ
                    e.tensor_mul(csc[0:1, h * NQ:(h + 1) * NQ],
                                 n2[0:1, qbase:qbase + NQ], n2[0:1, kbase:kbase + NQ])
                s_ve.inc(e.engine_nop(), 1)

            @b.scalar
            def _(e):
                e.wait_ge(s_ve.h, s_ve.v)
                e.activation(csc[0:1, 2 * NQ:4 * NQ], csc[0:1, 0:2 * NQ], AF.Ln)
                s_se.inc(e.activation(csc[0:1, 0:2 * NQ], csc[0:1, 2 * NQ:4 * NQ],
                                      AF.Exp, scale=0.5), 1)

            @b.vector
            def _(e):
                e.wait_ge(s_se.h, s_se.v)
                s_ve.inc(e.tensor_scalar(csc[0:1, 0:2 * NQ], csc[0:1, 0:2 * NQ],
                                         -RSQ, -1.0, ALU.mult, ALU.add), 1)

        for h in range(2):
            for qc in range(NQ):
                nkb = (qc + 1) * 4
                cpar = (h * NQ + qc) % 2
                with nc.Block() as b:
                    @b.tensor
                    def _(e, h=h, qc=qc, cpar=cpar):
                        e.wait_ge(s_ex.h, s_ex.v)
                        e.matmul(p_cb[:, cpar:cpar + 1], onesrow[:],
                                 csc[0:1, h * NQ + qc:h * NQ + qc + 1],
                                 start=True, stop=True)
                        s_pe.inc(e.matmul(p_cb[:, 2:3], onesrow[:], csc[0:1, 0:1],
                                          start=True, stop=True), 1)

                    @b.vector
                    def _(e, cpar=cpar):
                        e.wait_ge(s_pe.h, s_pe.v)
                        s_ve.inc(e.tensor_copy(cbias[:, cpar:cpar + 1],
                                               p_cb[:, cpar:cpar + 1]), 1)

                ex_base = s_ex.v
                pv_base = s_pv.v
                pe_base = s_pe.v
                cb_ve = s_ve.v
                for kb in range(nkb):
                    par = kb % 2
                    diag = (kb * 128) >= (qc * 512)
                    with nc.Block() as b:
                        @b.tensor
                        def _(e, h=h, qc=qc, kb=kb, par=par, ex_base=ex_base, cb_ve=cb_ve):
                            if kb >= 2:
                                e.wait_ge(s_ex.h, ex_base + kb - 1)
                            if kb == 0:
                                e.wait_ge(s_ve.h, cb_ve)
                            s_pe.inc(e.matmul(
                                p_s[:, par * 512:(par + 1) * 512],
                                KT[:, h * T + kb * 128:h * T + (kb + 1) * 128],
                                QT[:, h * T + qc * 512:h * T + (qc + 1) * 512],
                                start=True, stop=True), 1)

                        @b.scalar
                        def _(e, kb=kb, par=par, cpar=cpar, pe_base=pe_base, pv_base=pv_base):
                            e.wait_ge(s_pe.h, pe_base + kb + 1)
                            if kb >= 2:
                                e.wait_ge(s_pv.h, pv_base + kb - 1)
                            ins = e.activation(PT[:, par * 512:(par + 1) * 512],
                                               p_s[:, par * 512:(par + 1) * 512],
                                               AF.Exp, bias=cbias[:, cpar:cpar + 1],
                                               scale=RSQ)
                            s_ex.inc(ins, 1)

                        if diag:
                            @b.vector
                            def _(e, qc=qc, kb=kb, par=par, ex_base=ex_base):
                                e.wait_ge(s_ex.h, ex_base + kb + 1)
                                r = kb * 128 - qc * 512
                                s_ve.inc(e.tensor_mul(PT[:, par * 512:(par + 1) * 512],
                                                      PT[:, par * 512:(par + 1) * 512],
                                                      msk[:, 384 - r:896 - r]), 1)

                        mask_ve = s_ve.v

                        @b.tensor
                        def _(e, h=h, kb=kb, par=par, nkb=nkb, diag=diag,
                              mask_ve=mask_ve, ex_base=ex_base):
                            e.wait_ge(s_ex.h, ex_base + kb + 1)
                            if diag:
                                e.wait_ge(s_ve.h, mask_ve)
                            e.matmul(p_l[:], ones128[:], PT[:, par * 512:(par + 1) * 512],
                                     start=(kb == 0), stop=(kb == nkb - 1))
                            s_pv.inc(e.matmul(p_o[:],
                                              VV[:, h * T + kb * 128:h * T + (kb + 1) * 128],
                                              PT[:, par * 512:(par + 1) * 512],
                                              start=(kb == 0), stop=(kb == nkb - 1)), 1)

                with nc.Block() as b:
                    @b.scalar
                    def _(e, nkb=nkb, pv_base=pv_base):
                        e.wait_ge(s_pv.h, pv_base + nkb)
                        e.wait_ge(s_o.h, s_o.v)
                        e.activation(lst[:], p_l[:], AF.Ln)
                        s_se.inc(e.activation(lst[:], lst[:], AF.Exp, scale=-1.0), 1)

                    @b.vector
                    def _(e):
                        e.wait_ge(s_se.h, s_se.v)
                        s_ve.inc(e.tensor_mul(ot[:], p_o[:], lst[:]), 1)

                    @b.sync
                    def _(e, h=h, qc=qc):
                        e.wait_ge(s_ve.h, s_ve.v)
                        s_o.inc(e.dma_start(
                            out=OT_o[h * 128:(h + 1) * 128, qc * 512:(qc + 1) * 512],
                            in_=ot[:]), 16)

        with nc.Block() as b:
            @b.sync
            def _(e):
                e.wait_ge(s_o.h, s_o.v)

    return nc


def build_l3():
    nc = bass.Bass("TRN2", target_bir_lowering=False)
    dram = lambda n, s, d, k="ExternalInput": nc.dram_tensor(n, s, d, kind=k)

    OTw = dram("OTw", [D, OWN], BF16)
    u2T = dram("u2T", [D, OWN], BF16)
    rrT = dram("rrT", [D, OWN], BF16)
    xpxn = dram("xpxn", [OWN, D], F32)
    WinT = dram("WinT", [D, F], F32)
    WupT = dram("WupT", [D, F], F32)
    WdownT = dram("WdownT", [F, D], F32)
    WoutT = dram("WoutT", [D, D], F32)
    Wo_n = dram("Wo_n", [D, D], F32)
    A_in = dram("A_in", [D, 16], F32)
    B_in = dram("B_in", [16, F], F32)
    mlph = dram("mlph", [128, (OWN // 128) * D], F32, "Internal")
    out_o = dram("out_o", [OWN, D], F32, "ExternalOutput")

    FS = 2048
    NFS = F // FS
    NFC = FS // 128
    NTC = OWN // 128

    from contextlib import ExitStack
    with ExitStack() as ctx:
        sb = lambda n, s, d: ctx.enter_context(nc.sbuf_tensor(n, s, d))

        Abf = sb("Abf", [128, NBK * 16], BF16)
        Apr = sb("Apr", [128, NBK * 16], BF16)
        Bbf = sb("Bbf", [16, F], BF16)
        modhA = sb("modhA", [16, OWN], BF16)
        mstg = sb("mstg", [16, 2 * OWN], F32)
        ostg = sb("ostg", [128, D], F32)

        s_x = Sem(nc, "s_x")
        s_w = Sem(nc, "s_w")
        s_pe = Sem(nc, "s_pe")
        s_ve = Sem(nc, "s_ve")
        s_se = Sem(nc, "s_se")
        s_o = Sem(nc, "s_o")

        p_a = ctx.enter_context(nc.psum_tensor("p3a", [128, 1024], F32))
        p_b = ctx.enter_context(nc.psum_tensor("p3b", [128, 1024], F32))
        p_d = ctx.enter_context(nc.psum_tensor("p3d", [128, 2048], F32))

        with nc.Block() as b:
            @b.gpsimd
            def _(e):
                s_w.inc(e.dma_start(out=r3(Abf.ap(), 16), in_=hb3(A_in[:, :])), 16)
                s_w.inc(e.dma_start(out=Bbf[:], in_=B_in[:, :]), 16)

        with ExitStack() as ca:
            wost = ca.enter_context(nc.sbuf_tensor("wost", [128, NBK * D], BF16))
            with nc.Block() as b:
                @b.gpsimd
                def _(e):
                    for ic in range(NBK):
                        s_w.inc(e.dma_start(out=wost[:, ic * D:(ic + 1) * D],
                                            in_=Wo_n[ic * 128:(ic + 1) * 128, :]), 16)

            for jc in range(NBK):
                pp = [p_a, p_b][jc % 2]
                with nc.Block() as b:
                    @b.tensor
                    def _(e, jc=jc, pp=pp):
                        e.wait_ge(s_w.h, s_w.v)
                        e.wait_ge(s_ve.h, s_ve.v)
                        for ic in range(NBK):
                            e.matmul(pp[:, 0:16],
                                     wost[:, ic * D + jc * 128:ic * D + (jc + 1) * 128],
                                     Abf[:, ic * 16:(ic + 1) * 16],
                                     start=(ic == 0), stop=(ic == NBK - 1))
                        s_pe.inc(e.matmul(pp[:, 16:32], wost[:, jc * 128:(jc + 1) * 128],
                                          Abf[:, 0:16], start=True, stop=True), 1)

                    @b.vector
                    def _(e, jc=jc, pp=pp):
                        e.wait_ge(s_pe.h, s_pe.v)
                        s_ve.inc(e.tensor_copy(Apr[:, jc * 16:(jc + 1) * 16], pp[:, 0:16]), 1)

        with ExitStack() as cu:
            u2f = cu.enter_context(nc.sbuf_tensor("u2f", [128, NBK * OWN], BF16))
            with nc.Block() as b:
                @b.sync
                def _(e):
                    for q in range(2):
                        s_x.inc(e.dma_start(
                            out=r3(u2f[:, q * 8 * OWN:(q + 1) * 8 * OWN], OWN),
                            in_=hb3(u2T[:, :])[:, q * 8:(q + 1) * 8, :]), 16)

            with ExitStack() as cm:
                otf = cm.enter_context(nc.sbuf_tensor("otf", [128, NBK * OWN], BF16))
                with nc.Block() as b:
                    @b.sync
                    def _(e):
                        for q in range(2):
                            s_x.inc(e.dma_start(
                                out=r3(otf[:, q * 8 * OWN:(q + 1) * 8 * OWN], OWN),
                                in_=hb3(OTw[:, :])[:, q * 8:(q + 1) * 8, :]), 16)

                for half in range(2):
                    for tc in range(OWN // 512):
                        pp = [p_a, p_b][tc % 2]
                        with nc.Block() as b:
                            @b.tensor
                            def _(e, half=half, tc=tc, pp=pp):
                                e.wait_ge(s_x.h, s_x.v)
                                e.wait_ge(s_ve.h, s_ve.v)
                                e.wait_ge(s_se.h, s_se.v)
                                lhs_t = Apr if half == 0 else Abf
                                src = otf if half == 0 else u2f
                                ins = None
                                for dc in range(NBK):
                                    ins = e.matmul(
                                        pp[0:16, 0:512],
                                        lhs_t[:, dc * 16:(dc + 1) * 16],
                                        src[:, dc * OWN + tc * 512:dc * OWN + (tc + 1) * 512],
                                        start=(dc == 0), stop=(dc == NBK - 1))
                                s_pe.inc(ins, 1)

                            @b.scalar
                            def _(e, half=half, tc=tc, pp=pp):
                                e.wait_ge(s_pe.h, s_pe.v)
                                if half == 0:
                                    ins = e.activation(mstg[0:16, tc * 512:(tc + 1) * 512],
                                                       pp[0:16, 0:512], AF.Tanh)
                                else:
                                    ins = e.activation(
                                        mstg[0:16, OWN + tc * 512:OWN + (tc + 1) * 512],
                                        pp[0:16, 0:512], AF.Identity)
                                s_se.inc(ins, 1)

                with nc.Block() as b:
                    @b.vector
                    def _(e):
                        e.wait_ge(s_se.h, s_se.v)
                        e.tensor_scalar_mul(mstg[0:16, 0:OWN], mstg[0:16, 0:OWN], MOD_SCALE)
                        s_ve.inc(e.tensor_mul(modhA[0:16, :], mstg[0:16, 0:OWN],
                                              mstg[0:16, OWN:2 * OWN]), 1)

            with ExitStack() as cw:
                wsup = cw.enter_context(nc.sbuf_tensor("wsup", [128, NBK * FS], BF16))
                gus = cw.enter_context(nc.sbuf_tensor("gus", [128, NFC * OWN], BF16))
                for fs in range(NFS):
                    with nc.Block() as b:
                        @b.gpsimd
                        def _(e, fs=fs):
                            e.wait_ge(s_pe.h, s_pe.v)
                            for q in range(4):
                                s_w.inc(e.dma_start(
                                    out=r3(wsup[:, q * 4 * FS:(q + 1) * 4 * FS], FS),
                                    in_=WinT[:, fs * FS:(fs + 1) * FS]
                                        .rearrange("(a p) t -> p a t", p=128)[:, q * 4:(q + 1) * 4, :]), 16)

                    for fc in range(NFC):
                        pp = [p_a, p_b][fc % 2]
                        with nc.Block() as b:
                            @b.tensor
                            def _(e, fs=fs, fc=fc, pp=pp):
                                e.wait_ge(s_w.h, s_w.v)
                                e.wait_ge(s_ve.h, s_ve.v)
                                e.wait_ge(s_se.h, s_se.v)
                                for dc in range(NBK):
                                    for tc in range(2):
                                        e.matmul(
                                            pp[:, tc * 512:(tc + 1) * 512],
                                            wsup[:, dc * FS + fc * 128:dc * FS + (fc + 1) * 128],
                                            u2f[:, dc * OWN + tc * 512:dc * OWN + (tc + 1) * 512],
                                            start=(dc == 0), stop=False)
                                ins = None
                                for tc in range(2):
                                    ins = e.matmul(
                                        pp[:, tc * 512:(tc + 1) * 512],
                                        Bbf[0:16, fs * FS + fc * 128:fs * FS + (fc + 1) * 128],
                                        modhA[0:16, tc * 512:(tc + 1) * 512],
                                        start=False, stop=True)
                                s_pe.inc(ins, 1)

                            @b.scalar
                            def _(e, fc=fc, pp=pp):
                                e.wait_ge(s_pe.h, s_pe.v)
                                s_se.inc(e.activation(gus[:, fc * OWN:(fc + 1) * OWN],
                                                      pp[:], AF.Silu), 1)

                    with nc.Block() as b:
                        @b.gpsimd
                        def _(e, fs=fs):
                            e.wait_ge(s_pe.h, s_pe.v)
                            for q in range(4):
                                s_w.inc(e.dma_start(
                                    out=r3(wsup[:, q * 4 * FS:(q + 1) * 4 * FS], FS),
                                    in_=WupT[:, fs * FS:(fs + 1) * FS]
                                        .rearrange("(a p) t -> p a t", p=128)[:, q * 4:(q + 1) * 4, :]), 16)

                    for fc in range(NFC):
                        pp = [p_a, p_b][fc % 2]
                        with nc.Block() as b:
                            @b.tensor
                            def _(e, fc=fc, pp=pp):
                                e.wait_ge(s_w.h, s_w.v)
                                e.wait_ge(s_ve.h, s_ve.v)
                                e.wait_ge(s_se.h, s_se.v)
                                ins = None
                                for dc in range(NBK):
                                    for tc in range(2):
                                        ins = e.matmul(
                                            pp[:, tc * 512:(tc + 1) * 512],
                                            wsup[:, dc * FS + fc * 128:dc * FS + (fc + 1) * 128],
                                            u2f[:, dc * OWN + tc * 512:dc * OWN + (tc + 1) * 512],
                                            start=(dc == 0), stop=(dc == NBK - 1))
                                s_pe.inc(ins, 1)

                            @b.vector
                            def _(e, fc=fc, pp=pp):
                                e.wait_ge(s_pe.h, s_pe.v)
                                s_ve.inc(e.tensor_mul(gus[:, fc * OWN:(fc + 1) * OWN],
                                                      gus[:, fc * OWN:(fc + 1) * OWN],
                                                      pp[:]), 1)

                    with nc.Block() as b:
                        @b.gpsimd
                        def _(e, fs=fs):
                            e.wait_ge(s_pe.h, s_pe.v)
                            for fc in range(NFC):
                                s_w.inc(e.dma_start(
                                    out=wsup[:, fc * FS:fc * FS + D],
                                    in_=WdownT[fs * FS + fc * 128:fs * FS + (fc + 1) * 128, :]), 16)

                    for tch in range(NTC):
                        with nc.Block() as b:
                            @b.tensor
                            def _(e, fs=fs, tch=tch):
                                e.wait_ge(s_w.h, s_w.v)
                                e.wait_ge(s_ve.h, s_ve.v)
                                ins = None
                                for fc in range(NFC):
                                    for jc in range(4):
                                        ins = e.matmul(
                                            p_d[:, jc * 512:(jc + 1) * 512],
                                            gus[:, fc * OWN + tch * 128:fc * OWN + (tch + 1) * 128],
                                            wsup[:, fc * FS + jc * 512:fc * FS + (jc + 1) * 512],
                                            start=(fc == 0), stop=(fc == NFC - 1))
                                s_pe.inc(ins, 1)

                            @b.vector
                            def _(e, fs=fs, tch=tch):
                                e.wait_ge(s_pe.h, s_pe.v)
                                e.wait_ge(s_o.h, s_o.v)
                                s_ve.inc(e.tensor_copy(ostg[:], p_d[:]), 1)

                            @b.gpsimd
                            def _(e, fs=fs, tch=tch):
                                e.wait_ge(s_ve.h, s_ve.v)
                                ins = e.dma_start(
                                    out=mlph[:, tch * D:(tch + 1) * D], in_=ostg[:],
                                    accum_op=(ALU.bypass if fs == 0 else ALU.add))
                                s_o.inc(ins, 16)

        with ExitStack() as cf:
            rrf = cf.enter_context(nc.sbuf_tensor("rrf", [128, NBK * OWN], BF16))
            wout = cf.enter_context(nc.sbuf_tensor("wout", [128, NBK * D], BF16))
            xpt = cf.enter_context(nc.sbuf_tensor("xpt", [128, 2 * D], F32))
            mlt = cf.enter_context(nc.sbuf_tensor("mlt", [128, 2 * D], F32))
            with nc.Block() as b:
                @b.sync
                def _(e):
                    for q in range(2):
                        s_x.inc(e.dma_start(
                            out=r3(rrf[:, q * 8 * OWN:(q + 1) * 8 * OWN], OWN),
                            in_=hb3(rrT[:, :])[:, q * 8:(q + 1) * 8, :]), 16)

                @b.gpsimd
                def _(e):
                    e.wait_ge(s_pe.h, s_pe.v)
                    for ac in range(NBK):
                        s_w.inc(e.dma_start(out=wout[:, ac * D:(ac + 1) * D],
                                            in_=WoutT[ac * 128:(ac + 1) * 128, :]), 16)

            for tch in range(NTC):
                xp_par = tch % 2
                with nc.Block() as b:
                    @b.sync
                    def _(e, tch=tch, xp_par=xp_par):
                        if tch >= 2:
                            e.wait_ge(s_ve.h, s_ve.v)
                        s_x.inc(e.dma_start(out=xpt[:, xp_par * D:(xp_par + 1) * D],
                                            in_=xpxn[tch * 128:(tch + 1) * 128, :]), 16)
                        s_x.inc(e.dma_start(out=mlt[:, xp_par * D:(xp_par + 1) * D],
                                            in_=mlph[:, tch * D:(tch + 1) * D]), 16)

                    @b.tensor
                    def _(e, tch=tch):
                        e.wait_ge(s_x.h, s_x.v)
                        e.wait_ge(s_w.h, s_w.v)
                        e.wait_ge(s_ve.h, s_ve.v)
                        ins = None
                        for ac in range(NBK):
                            for jc in range(4):
                                ins = e.matmul(
                                    p_d[:, jc * 512:(jc + 1) * 512],
                                    rrf[:, ac * OWN + tch * 128:ac * OWN + (tch + 1) * 128],
                                    wout[:, ac * D + jc * 512:ac * D + (jc + 1) * 512],
                                    start=(ac == 0), stop=(ac == NBK - 1))
                        s_pe.inc(ins, 1)

                    @b.vector
                    def _(e, tch=tch, xp_par=xp_par):
                        e.wait_ge(s_pe.h, s_pe.v)
                        e.wait_ge(s_x.h, s_x.v)
                        e.wait_ge(s_o.h, s_o.v)
                        e.tensor_add(ostg[:], mlt[:, xp_par * D:(xp_par + 1) * D], p_d[:])
                        s_ve.inc(e.tensor_add(ostg[:], ostg[:],
                                              xpt[:, xp_par * D:(xp_par + 1) * D]), 1)

                    @b.sync
                    def _(e, tch=tch):
                        e.wait_ge(s_ve.h, s_ve.v)
                        s_o.inc(e.dma_start(out=out_o[tch * 128:(tch + 1) * 128, :],
                                            in_=ostg[:]), 16)

        with nc.Block() as b:
            @b.sync
            def _(e):
                e.wait_ge(s_o.h, s_o.v)

    return nc


def _cm(v):
    return np.ascontiguousarray(v.reshape(NBK, 128).T)


_L1_NC = None
_L2_NC = None
_L3_NC = None


def _get_l1():
    global _L1_NC
    if _L1_NC is None:
        _L1_NC = build_l1()
    return _L1_NC


def run_l1(inputs):
    x = np.asarray(inputs["L0_attn_hidden"])[0]
    xT = np.ascontiguousarray(x.T)
    WkeyT = np.ascontiguousarray(np.asarray(inputs["L0_attn_W_key"]).T)
    WvalT = np.ascontiguousarray(np.asarray(inputs["L0_attn_W_value"]).T)
    WrecT = np.ascontiguousarray(np.asarray(inputs["L0_attn_W_receptance"]).T)
    WqT = np.ascontiguousarray(np.asarray(inputs["L0_ffn_Wq"]).T)
    WkT = np.ascontiguousarray(np.asarray(inputs["L0_ffn_Wk"]).T)
    WvT = np.ascontiguousarray(np.asarray(inputs["L0_ffn_Wv"]).T)

    vec = np.zeros((128, NBK * 8), np.float32)
    vec[:, 0 * NBK:1 * NBK] = _cm(np.asarray(inputs["L0_attn_time_mix_key"]).reshape(D))
    vec[:, 1 * NBK:2 * NBK] = _cm(np.asarray(inputs["L0_attn_time_mix_value"]).reshape(D))
    vec[:, 2 * NBK:3 * NBK] = _cm(np.asarray(inputs["L0_attn_time_mix_receptance"]).reshape(D))
    vec[:, 3 * NBK:4 * NBK] = _cm(np.asarray(inputs["L0_attn_time_first"]).reshape(D))
    vec[:, 4 * NBK:5 * NBK] = _cm(np.asarray(inputs["L0_attn_time_decay"]).reshape(D))
    vec[:, 5 * NBK:6 * NBK] = _cm(np.asarray(inputs["L0_ffn_ln1_w"]).reshape(D))
    vec[:, 6 * NBK:7 * NBK] = _cm(np.asarray(inputs["L0_ffn_ln2_w"]).reshape(D))

    in_maps = []
    for c in range(NC):
        s = c * OWN
        lb0 = s - LB
        if lb0 >= 0:
            xw = xT[:, lb0:s + OWN]
            m1 = xT[:, lb0 - 1] if lb0 - 1 >= 0 else np.zeros(D, np.float32)
            lbv = 1.0
            m1v = 1.0 if lb0 - 1 >= 0 else 0.0
        else:
            xw = np.concatenate([np.zeros((D, LB), np.float32), xT[:, s:s + OWN]], axis=1)
            m1 = np.zeros(D, np.float32)
            lbv = 0.0
            m1v = 0.0
        in_maps.append(dict(
            xT=np.ascontiguousarray(xw), xTm1=_cm(m1) * np.float32(m1v),
            WkeyT=WkeyT, WvalT=WvalT, WrecT=WrecT, WqT=WqT, WkT=WkT, WvT=WvT,
            VEC=vec,
            lbm=np.ascontiguousarray(np.broadcast_to(
                np.array([lbv, (lbv - 1.0) * 1e38], np.float32), (128, 2)))))
    res = run_bass_kernel_spmd(_get_l1(), in_maps, core_ids=list(range(NC)))
    return res.results


_L2_NC = None
_L3_NC = None


def _get_l2():
    global _L2_NC
    if _L2_NC is None:
        _L2_NC = build_l2()
    return _L2_NC


def _get_l3():
    global _L3_NC
    if _L3_NC is None:
        _L3_NC = build_l3()
    return _L3_NC


_CMASK = None


def _cmask():
    global _CMASK
    if _CMASK is None:
        _CMASK = (np.arange(896)[None, :] >=
                  (np.arange(128)[:, None] + 384)).astype(BF16NP)
    return _CMASK


def kernel(**inputs):
    r1 = run_l1(inputs)

    KT_full = np.concatenate([np.asarray(r["KT_o"]) for r in r1], axis=1)
    QT_full = np.concatenate([np.asarray(r["QT_o"]) for r in r1], axis=1)
    V_full = np.concatenate([np.asarray(r["V_o"]) for r in r1], axis=0)
    cm = _cmask()

    in2 = []
    for c in range(NC):
        sl = slice(c * 256, (c + 1) * 256)
        in2.append(dict(KTs=np.ascontiguousarray(KT_full[sl]),
                        QTs=np.ascontiguousarray(QT_full[sl]),
                        Vs=np.ascontiguousarray(V_full[:, sl]),
                        cmask=cm))
    r2 = run_bass_kernel_spmd(_get_l2(), in2, core_ids=list(range(NC))).results
    OT_full = np.concatenate([np.asarray(r["OT_o"]) for r in r2], axis=0)

    WinT = np.ascontiguousarray(np.asarray(inputs["L0_ffn_W_in_base"]).T)
    WupT = np.ascontiguousarray(np.asarray(inputs["L0_ffn_W_up_base"]).T)
    WdownT = np.ascontiguousarray(np.asarray(inputs["L0_ffn_W_down"]).T)
    WoutT = np.ascontiguousarray(np.asarray(inputs["L0_attn_W_output"]).T)
    Wo_n = np.ascontiguousarray(np.asarray(inputs["L0_ffn_Wo"]))
    A_in = np.ascontiguousarray(np.asarray(inputs["L0_ffn_A"]))
    B_in = np.ascontiguousarray(np.asarray(inputs["L0_ffn_B"]))

    in3 = []
    for c in range(NC):
        s = c * OWN
        in3.append(dict(
            OTw=np.ascontiguousarray(OT_full[:, s:s + OWN]),
            u2T=np.asarray(r1[c]["u2T_o"]),
            rrT=np.asarray(r1[c]["rwkvrT_o"]),
            xpxn=np.ascontiguousarray(np.asarray(r1[c]["xpxnT_o"]).T),
            WinT=WinT, WupT=WupT, WdownT=WdownT, WoutT=WoutT, Wo_n=Wo_n,
            A_in=A_in, B_in=B_in))
    r3_ = run_bass_kernel_spmd(_get_l3(), in3, core_ids=list(range(NC))).results
    out = np.concatenate([np.asarray(r["out_o"]) for r in r3_], axis=0)
    return out.reshape(1, T, D).astype(np.float32)
